# version 09
# baseline (speedup 1.0000x reference)
import numpy as np
import concourse.bass as bass
import concourse.mybir as mybir
from concourse.bass_utils import run_bass_kernel_spmd

F32 = mybir.dt.float32
BF16 = mybir.dt.bfloat16
AF = mybir.ActivationFunctionType
ALU = mybir.AluOpType
AX = mybir.AxisListType

D = 2048
DC = 16
S = 2048
TOWN = 1024
H = 16
G = 4
HD = 128
NE = 32
ICH = 6
TAU = 0.05
EPS = 1e-5


class Builder:
    ENGINES = ("tensor", "vector", "scalar", "sync", "gpsimd")

    def __init__(self, nc):
        self.nc = nc
        self.streams = {e: [] for e in self.ENGINES}
        self.sems = {}
        self.cnt = {}
        names = ["st", "sv", "ss", "sc", "sgc",
                 "sxt0", "sxt1", "swj0", "swj1", "swo0", "swo1",
                 "sgu0", "sgu1", "sgu2", "sgu3", "swd0", "swd1", "swd2",
                 "ssel0", "ssel1", "sxr0", "sxr1", "sou0", "sou1"]
        for name in names:
            self.sems[name] = nc.alloc_semaphore(name)
            self.cnt[name] = 0

    SELF_SEM = {"vector": "sv", "scalar": "ss"}

    def op(self, eng, fn, inc=None, n=1):
        auto = self.SELF_SEM.get(eng)
        if auto is not None:
            assert inc in (None, auto), (eng, inc)
            self.wait(eng, auto, self.cnt[auto])
            inc = auto
            n = 1
        if inc is None:
            self.streams[eng].append(lambda e, fn=fn: fn(e))
            return None
        self.cnt[inc] += n
        sem = self.sems[inc]
        self.streams[eng].append(
            lambda e, fn=fn, sem=sem, n=n: fn(e).then_inc(sem, n))
        return self.cnt[inc]

    def dma(self, eng, out, in_, inc):
        return self.op(eng, lambda e, o=out, i=in_: e.dma_start(o, i), inc, 16)

    def wait(self, eng, semname, val):
        if val is None or val <= 0:
            return
        self.streams[eng].append(
            lambda e, s=self.sems[semname], v=val: e.wait_ge(s, v))

    def barrier(self):
        snap = {k: self.cnt[k] for k in ("st", "sv", "ss")}
        for eng in self.ENGINES:
            for k, v in snap.items():
                self.wait(eng, k, v)

    def finalize(self):
        nc = self.nc
        with nc.Block() as block:
            @block.tensor
            def _(t):
                for f in self.streams["tensor"]:
                    f(t)

            @block.vector
            def _(v):
                for f in self.streams["vector"]:
                    f(v)

            @block.scalar
            def _(s):
                for f in self.streams["scalar"]:
                    f(s)

            @block.sync
            def _(sy):
                for f in self.streams["sync"]:
                    f(sy)

            @block.gpsimd
            def _(g):
                for f in self.streams["gpsimd"]:
                    f(g)


def build_program():
    nc = bass.Bass("TRN2", target_bir_lowering=False)

    dx_own = nc.dram_tensor("x_own", [TOWN, D], F32, kind="ExternalInput").ap()
    dx_nbr = nc.dram_tensor("x_nbr", [TOWN, D], F32, kind="ExternalInput").ap()
    dwqkv = nc.dram_tensor("wqkv3", [24, 128, DC, 128], F32, kind="ExternalInput").ap()
    dwout = nc.dram_tensor("wout3", [DC, 128, DC, 128], F32, kind="ExternalInput").ap()
    dgate = nc.dram_tensor("gate3", [128, DC, NE], F32, kind="ExternalInput").ap()
    dbqkv = nc.dram_tensor("bqkv2", [128, 24], F32, kind="ExternalInput").ap()
    dbout = nc.dram_tensor("bout2", [128, 16], F32, kind="ExternalInput").ap()
    dalpha = nc.dram_tensor("alpha128", [128, 1], F32, kind="ExternalInput").ap()
    dgu = nc.dram_tensor("gu4", [NE, 12, 128, DC, 128], F32, kind="ExternalInput").ap()
    ddown = nc.dram_tensor("down4", [NE, 128, DC, ICH, 128], F32, kind="ExternalInput").ap()
    dsel = nc.dram_tensor("sel3", [NE, NE, 128], F32, kind="ExternalInput").ap()
    dident = nc.dram_tensor("identm", [128, 128], F32, kind="ExternalInput").ap()
    dout = nc.dram_tensor("o", [TOWN, D], F32, kind="ExternalOutput").ap()

    MBASE = 16512

    def sb(name, shape, dtype, off):
        return nc.alloc_sbuf_tensor_at(name, shape, dtype, offset=MBASE + off).ap()

    ident_f = sb("ident_f", [128, 128], F32, 0)
    ident_b = sb("ident_b", [128, 128], BF16, 512)
    gateT = sb("gateT", [128, DC * NE], F32, 768)
    bqkv_sb = sb("bqkv_sb", [128, 24], F32, 2816)
    bout_sb = sb("bout_sb", [128, 16], F32, 2912)
    alpha_sb = sb("alpha_sb", [128, 1], F32, 2976)
    selbuf = [sb(f"selbuf{i}", [NE, 128], F32, 3008 + i * 512) for i in range(2)]
    combineT = sb("combineT", [NE, TOWN], F32, 4032)

    class Small:
        def __init__(self, base, limit):
            self.off = base
            self.limit = limit

        def get(self, name, cols, dtype=F32):
            nbytes = cols * (4 if dtype == F32 else 2)
            nbytes = (nbytes + 31) & ~31
            t = sb(name, [128, cols], dtype, self.off)
            self.off += nbytes
            assert self.off <= self.limit, name
            return t

    sm = Small(8192, 16384)
    nm8 = None
    sumx = sm.get("sumx", 1)
    sumx2 = sm.get("sumx2", 1)
    mean = sm.get("mean", 1)
    t1s = sm.get("t1s", 1)
    m2s = sm.get("m2s", 1)
    vars_ = sm.get("vars_", 1)
    rstd = sm.get("rstd", 1)
    n1s = sm.get("n1s", 1)
    n3s = sm.get("n3s", 1)
    negmr = sm.get("negmr", 1)
    nmax = sm.get("nmax", 1)
    nmax128 = sm.get("nmax128", 1)
    ssum = sm.get("ssum", 1)
    ssum2 = [sm.get("ssum2a", 1), sm.get("ssum2b", 1)]
    lns2 = [sm.get("lns2a", 1), sm.get("lns2b", 1)]
    nmax2 = [sm.get("nmax2a", 1), sm.get("nmax2b", 1)]
    rssum = sm.get("rssum", 1)
    anyact = sm.get("anyact", 1)
    empty = sm.get("empty", 1)
    spos = sm.get("spos", 1)
    sneg = sm.get("sneg", 1)
    sum8 = sm.get("sum8", 1)
    rs8 = sm.get("rs8", 1)
    dens = sm.get("dens", 1)
    rdens = sm.get("rdens", 1)
    logits_sb = sm.get("logits_sb", NE)
    p32 = sm.get("p32", NE)
    rank = sm.get("rank", NE)
    mask8 = sm.get("mask8", NE)
    pm = sm.get("pm", NE)
    wts = sm.get("wts", NE)
    active = sm.get("active", NE)
    e0m = sm.get("e0m", NE)
    e7m = sm.get("e7m", NE)
    fbm = sm.get("fbm", NE)
    fb2m = sm.get("fb2m", NE)
    act2 = sm.get("act2", NE)
    rwm = sm.get("rwm", NE)
    comb = sm.get("comb", NE)
    sumx_all = sm.get("sumx_all", 16)
    sumx2_all = sm.get("sumx2_all", 16)
    mean_all = sm.get("mean_all", 16)
    t1a = sm.get("t1a", 16)
    m2a = sm.get("m2a", 16)
    var_all = sm.get("var_all", 16)
    rstd_all = sm.get("rstd_all", 16)
    n1a = sm.get("n1a", 16)
    n3a = sm.get("n3a", 16)
    negmr_all = sm.get("negmr_all", 16)
    bqkv128 = sm.get("bqkv128", 16)
    nm8 = sm.get("nm8", 8)
    s8 = sm.get("s8", 8)
    rs8v = sm.get("rs8v", 8)
    any8 = sm.get("any8", 8)
    emp8 = sm.get("emp8", 8)
    den8 = sm.get("den8", 8)
    rd8 = sm.get("rd8", 8)

    DYN = 16384

    xtile = [sb(f"xtile{i}", [128, D], F32, DYN + i * 8192) for i in range(2)]
    xnbf = [sb(f"xnbf{i}", [128, D], BF16, 32768 + i * 4096) for i in range(2)]
    xnf = [sb(f"xnf{i}", [128, D], F32, 40960 + i * 8192) for i in range(2)]
    xnTf = [sb(f"xnTf{i}", [128, D], F32, 57344 + i * 8192) for i in range(2)]
    la_all = sb("la_all", [128, 8 * NE], F32, 139264)
    p_all = sb("p_all", [128, 8 * NE], F32, 140288)
    rank_all = sb("rank_all", [128, 8 * NE], F32, 141312)
    mask_all = sb("mask_all", [128, 8 * NE], F32, 142336)
    w_all = sb("w_all", [128, 8 * NE], F32, 143360)
    act_all = sb("act_all", [128, 8 * NE], F32, 144384)
    scr_all = sb("scr_all", [128, 8 * NE], F32, 145408)
    fb_all = sb("fb_all", [128, 8 * NE], F32, 146432)
    xn_own = sb("xn_own", [128, DC * TOWN], BF16, 73728)
    xn_nbr = sb("xn_nbr", [128, DC * TOWN], BF16, 106496)

    wjbuf = [sb(f"wjbuf{i}", [128, DC * 128], BF16, DYN + i * 4096) for i in range(2)]
    vstage = [sb(f"vstage{i}", [128, S], BF16, 24576 + i * 4096) for i in range(2)]
    q_sb = sb("q_sb", [128, H * TOWN], BF16, 139264)
    k_sb = sb("k_sb", [128, G * S], BF16, 172032)
    vt_sb = sb("vt_sb", [128, G * S], BF16, 188416)

    p_exp = [sb("p_exp0", [128, S], F32, DYN), sb("p_exp1", [128, S], F32, 65536)]
    pn_sb = [sb("pn_sb0", [128, S], BF16, 24576), sb("pn_sb1", [128, S], BF16, 61440)]
    attn_o = sb("attn_o", [128, H * TOWN], BF16, 28672)
    pnT = sb("pnT", [128, DC * TOWN], BF16, 106496)

    woutbuf = [sb(f"woutbuf{i}", [128, DC * 128], BF16, DYN + i * 4096) for i in range(2)]
    ffn_acc = sb("ffn_acc", [128, DC * TOWN], F32, 139264)
    wdbuf = [sb(f"wdbuf{i}", [128, 8 * ICH * 128], BF16, 16384 + i * 12288) for i in range(3)]
    wgubuf = [sb(f"wgubuf{i}", [128, DC * 128], BF16, 53248 + i * 4096) for i in range(4)]
    c_rep = [sb("c_rep0", [128, TOWN], F32, 69632), sb("c_rep1", [128, TOWN], F32, 106496)]
    sg_sb = sb("sg_sb", [128, TOWN], F32, 110592)
    hstash = [sb(f"hstash{i}", [128, ICH * TOWN], BF16, 114688 + i * 12288) for i in range(2)]
    x_re = [sb(f"x_re{i}", [128, D], F32, DYN + i * 8192) for i in range(2)]
    out_sb = [sb(f"out_sb{i}", [128, D], F32, 32768 + i * 8192) for i in range(2)]

    nc.bump_sbuf(MBASE + 204800 - nc.sbuf_base)

    ps_a = nc.alloc_psum_tensor("ps_a", [128, 2048], F32).ap()
    ps_b = nc.alloc_psum_tensor("ps_b", [128, 1024], BF16).ap()
    ps_c = nc.alloc_psum_tensor("ps_c", [128, 512], F32).ap()
    ps_d = nc.alloc_psum_tensor("ps_d", [128, 1024], F32).ap()
    ps_b2 = ps_d[:, 512:1024].bitcast(BF16)

    B = Builder(nc)

    B.dma("sync", ident_f[:], dident, "sc")
    B.dma("sync", gateT[:], dgate.rearrange("p a b -> p (a b)"), "sc")
    B.dma("sync", bqkv_sb[:], dbqkv, "sc")
    B.dma("sync", bout_sb[:], dbout, "sc")
    ev_consts = B.dma("sync", alpha_sb[:], dalpha, "sc")
    ev_identb = B.dma("gpsimd", ident_b[:], dident, "sgc")

    B.wait("vector", "sc", ev_consts)
    B.wait("scalar", "sc", ev_consts)
    B.wait("tensor", "sc", ev_consts)
    B.wait("tensor", "sgc", ev_identb)
    B.op("vector", lambda e: e.tensor_scalar(spos[:], alpha_sb[:], 0.0, None, ALU.is_ge))
    B.op("vector", lambda e: e.tensor_scalar(sneg[:], spos[:], -1.0, 1.0, ALU.mult, ALU.add), "sv")

    B.op("vector", lambda e: e.tensor_scalar_mul(bqkv128[:], bqkv_sb[:, 0:16], 1.0 / HD), "sv")

    ev_xdma = [None] * 16
    ev_red1 = [None] * 16
    ev_sq1 = [None] * 16
    ev_xn_done = [None] * 16
    ev_xnbf = [None] * 16
    ev_xnf = [None] * 16
    ev_tpb = [[None, None] for _ in range(16)]
    ev_tpb_evac = [[None, None] for _ in range(16)]
    ev_tpf = [[None, None] for _ in range(16)]
    ev_tpf_evac = [[None, None] for _ in range(16)]
    ev_gate = [None] * 16
    ev_cT = [None] * 16
    ev_cTev = [None] * 16

    for t in range(16):
        own = t < 8
        buf = t % 2
        src_ = dx_own if own else dx_nbr
        row = (t % 8) * 128
        if t >= 2:
            B.wait("sync", "sv", ev_red1[t - 2])
            B.wait("sync", "ss", ev_sq1[t - 2])
        ev_xdma[t] = B.dma("sync", xtile[buf][:], src_[row:row + 128, :], f"sxt{buf}")
        B.wait("vector", f"sxt{buf}", ev_xdma[t])
        ev_red1[t] = B.op("vector", lambda e, b=buf, t=t: e.reduce_sum(
            sumx_all[:, t:t + 1], xtile[b][:], axis=AX.X), "sv")
        B.wait("scalar", f"sxt{buf}", ev_xdma[t])
        ev_sq1[t] = B.op("scalar", lambda e, b=buf, t=t: e.activation(
            ps_a[:, 0:2048], xtile[b][:], AF.Square, accum_out=sumx2_all[:, t:t + 1]), "ss")

    B.wait("vector", "ss", ev_sq1[15])
    B.op("vector", lambda e: e.tensor_scalar_mul(mean_all[:], sumx_all[:], 1.0 / D))
    B.op("vector", lambda e: e.tensor_scalar_mul(t1a[:], sumx2_all[:], 1.0 / D))
    B.op("vector", lambda e: e.tensor_tensor(m2a[:], mean_all[:], mean_all[:], ALU.mult))
    B.op("vector", lambda e: e.tensor_tensor(var_all[:], t1a[:], m2a[:], ALU.subtract))
    B.op("vector", lambda e: e.tensor_scalar_add(var_all[:], var_all[:], EPS))
    ev_v1 = B.op("vector", lambda e: e.reciprocal(rstd_all[:], var_all[:]), "sv")
    B.wait("scalar", "sv", ev_v1)
    ev_s1 = B.op("scalar", lambda e: e.activation(rstd_all[:], rstd_all[:], AF.Sqrt), "ss")
    B.wait("vector", "ss", ev_s1)
    B.op("vector", lambda e: e.tensor_tensor(n1a[:], rstd_all[:], rstd_all[:], ALU.mult))
    B.op("vector", lambda e: e.tensor_tensor(n1a[:], n1a[:], var_all[:], ALU.mult))
    B.op("vector", lambda e: e.tensor_scalar(n3a[:], n1a[:], -0.5, 1.5, ALU.mult, ALU.add))
    B.op("vector", lambda e: e.tensor_tensor(rstd_all[:], rstd_all[:], n3a[:], ALU.mult))
    B.op("vector", lambda e: e.tensor_tensor(negmr_all[:], mean_all[:], rstd_all[:], ALU.mult))
    ev_ln = B.op("vector", lambda e: e.tensor_scalar_mul(negmr_all[:], negmr_all[:], -1.0), "sv")
    B.wait("scalar", "sv", ev_ln)

    for t in range(16):
        own = t < 8
        buf = t % 2
        src_ = dx_own if own else dx_nbr
        row = (t % 8) * 128
        if t >= 2:
            B.wait("sync", "ss", ev_xn_done[t - 2])
        else:
            B.wait("sync", "sv", ev_red1[14 + t])
            B.wait("sync", "ss", ev_sq1[14 + t])
        ev_xdma[t] = B.dma("sync", xtile[buf][:], src_[row:row + 128, :], f"sxt{buf}")

        B.wait("scalar", f"sxt{buf}", ev_xdma[t])
        if t >= 2:
            B.wait("scalar", "st", ev_tpb[t - 2][1])
            if t - 2 < 8:
                B.wait("scalar", "st", ev_tpf[t - 2][1])
        ev_xnbf[t] = B.op("scalar", lambda e, b=buf, t=t: e.activation(
            xnbf[b][:], xtile[b][:], AF.Identity,
            bias=negmr_all[:, t:t + 1], scale=rstd_all[:, t:t + 1]), "ss")
        if own:
            ev_xnf[t] = B.op("scalar", lambda e, b=buf, t=t: e.activation(
                xnf[b][:], xtile[b][:], AF.Identity,
                bias=negmr_all[:, t:t + 1], scale=rstd_all[:, t:t + 1]), "ss")
        ev_xn_done[t] = ev_xnf[t] if own else ev_xnbf[t]

        xn_dst = xn_own if own else xn_nbr
        B.wait("tensor", "ss", ev_xnbf[t])
        for k in range(2):
            if not (t == 0 and k == 0):
                prev = ev_tpb_evac[t][0] if k == 1 else ev_tpb_evac[t - 1][1]
                B.wait("tensor", "sv", prev)

            def tp_batch(e, b=buf, k=k):
                for j in range(8):
                    ins = e.matmul(ps_b[:, j * 128:(j + 1) * 128],
                                   xnbf[b][:, (8 * k + j) * 128:(8 * k + j + 1) * 128],
                                   ident_b[:], is_transpose=True,
                                   start=(j == 0), stop=(j == 7))
                return ins
            ev_tpb[t][k] = B.op("tensor", tp_batch, "st")
            B.wait("vector", "st", ev_tpb[t][k])
            dst_view = xn_dst.rearrange("p (c q) -> p c q", c=DC)[
                :, 8 * k:8 * k + 8, (t % 8) * 128:(t % 8) * 128 + 128]
            src_view = ps_b.rearrange("p (c q) -> p c q", c=8)
            ev_tpb_evac[t][k] = B.op("vector", lambda e, d=dst_view, s=src_view:
                                     e.tensor_copy(d, s), "sv")

        if own:
            B.wait("tensor", "ss", ev_xnf[t])
            if t >= 2:
                B.wait("scalar", "st", ev_gate[t - 2])
            for k in range(2):
                if not (t == 0 and k == 0):
                    prev = ev_tpf_evac[t][0] if k == 1 else \
                        (ev_tpf_evac[t - 1][1] if t >= 1 else None)
                    B.wait("tensor", "ss", prev)

                def tpf_batch(e, b=buf, k=k):
                    for j in range(8):
                        ins = e.matmul(ps_d[:, j * 128:(j + 1) * 128],
                                       xnf[b][:, (8 * k + j) * 128:(8 * k + j + 1) * 128],
                                       ident_f[:], is_transpose=True,
                                       start=(j % 4 == 0), stop=(j % 4 == 3))
                    return ins
                ev_tpf[t][k] = B.op("tensor", tpf_batch, "st")
                B.wait("scalar", "st", ev_tpf[t][k])
                ev_tpf_evac[t][k] = B.op("scalar", lambda e, b=buf, k=k: e.activation(
                    xnTf[b][:, k * 1024:(k + 1) * 1024], ps_d[:], AF.Copy), "ss")

            B.wait("tensor", "ss", ev_tpf_evac[t][1])
            if t >= 1:
                B.wait("tensor", "sv", ev_cTev[t - 1])

            def gate_mm(e, b=buf):
                for dc in range(DC):
                    ins = e.matmul(ps_c[:, 0:32],
                                   xnTf[b][:, dc * 128:(dc + 1) * 128],
                                   gateT[:, dc * NE:(dc + 1) * NE],
                                   start=(dc == 0), stop=(dc == DC - 1))
                return ins
            ev_gate[t] = B.op("tensor", gate_mm, "st")

            B.wait("vector", "st", ev_gate[t])
            ev_cTev[t] = B.op("vector", lambda e, t=t: e.tensor_copy(
                la_all[:, t * NE:(t + 1) * NE], ps_c[:, 0:32]), "sv")

        if t == 7:
            def V3(ap):
                return ap.rearrange("p (t e) -> p t e", t=8)

            def B1(ap8):
                return ap8.unsqueeze(2).broadcast_to([128, 8, NE])

            B.op("vector", lambda e: e.reduce_max(nm8[:], V3(la_all[:]), axis=AX.X, negate=True))
            B.op("vector", lambda e: e.tensor_tensor(V3(scr_all[:]), V3(la_all[:]), B1(nm8[:]), ALU.add))
            ev_sub = B.op("vector", lambda e: e.memset(rank_all[:], 0.0), "sv")
            B.wait("scalar", "sv", ev_sub)
            ev_pall = B.op("scalar", lambda e: e.activation(p_all[:], scr_all[:], AF.Exp), "ss")
            B.wait("vector", "ss", ev_pall)
            for ep in range(NE):
                B.op("vector", lambda e, ep=ep: e.tensor_tensor(
                    V3(scr_all[:]), V3(la_all[:]),
                    la_all[:].rearrange("p (t e) -> p t e", t=8)[:, :, ep:ep + 1].broadcast_to([128, 8, NE]),
                    ALU.is_lt))
                B.op("vector", lambda e: e.tensor_tensor(rank_all[:], rank_all[:], scr_all[:], ALU.add))
            B.op("vector", lambda e: e.tensor_scalar(mask_all[:], rank_all[:], 8.0, None, ALU.is_lt))
            B.op("vector", lambda e: e.tensor_tensor(scr_all[:], p_all[:], mask_all[:], ALU.mult))
            B.op("vector", lambda e: e.reduce_sum(s8[:], V3(scr_all[:]), axis=AX.X))
            B.op("vector", lambda e: e.tensor_scalar_max(s8[:], s8[:], 1e-30))
            B.op("vector", lambda e: e.reciprocal(rs8v[:], s8[:]))
            B.op("vector", lambda e: e.tensor_tensor(V3(w_all[:]), V3(scr_all[:]), B1(rs8v[:]), ALU.mult))
            B.op("vector", lambda e: e.tensor_scalar(
                act_all[:], w_all[:], alpha_sb[:], TAU, ALU.mult, ALU.is_ge))
            B.op("vector", lambda e: e.reduce_max(any8[:], V3(act_all[:]), axis=AX.X))
            B.op("vector", lambda e: e.tensor_scalar(
                emp8[:], any8[:], -1.0, 1.0, ALU.mult, ALU.add))
            B.op("vector", lambda e: e.tensor_scalar(
                fb_all[:], rank_all[:], 0.5, spos[:], ALU.is_lt, ALU.mult))
            B.op("vector", lambda e: e.tensor_scalar(
                scr_all[:], rank_all[:], 6.5, sneg[:], ALU.is_gt, ALU.mult))
            B.op("vector", lambda e: e.tensor_tensor(scr_all[:], scr_all[:], mask_all[:], ALU.mult))
            B.op("vector", lambda e: e.tensor_tensor(fb_all[:], fb_all[:], scr_all[:], ALU.add))
            B.op("vector", lambda e: e.tensor_tensor(V3(fb_all[:]), V3(fb_all[:]), B1(emp8[:]), ALU.mult))
            B.op("vector", lambda e: e.tensor_tensor(act_all[:], act_all[:], fb_all[:], ALU.add))
            B.op("vector", lambda e: e.tensor_tensor(mask_all[:], w_all[:], act_all[:], ALU.mult))
            B.op("vector", lambda e: e.reduce_sum(den8[:], V3(mask_all[:]), axis=AX.X))
            B.op("vector", lambda e: e.tensor_scalar_max(den8[:], den8[:], 1e-12))
            B.op("vector", lambda e: e.reciprocal(rd8[:], den8[:]))
            ev_comb_all = B.op("vector", lambda e: e.tensor_tensor(
                V3(w_all[:]), V3(mask_all[:]), B1(rd8[:]), ALU.mult), "sv")

            B.wait("tensor", "sv", ev_comb_all)
            prev_ctev = ev_cTev[7]
            for tt in range(8):
                B.wait("tensor", "sv", prev_ctev) if tt == 0 else B.wait("tensor", "ss", prev_ctev)
                ev_cT[tt] = B.op("tensor", lambda e, tt=tt: e.matmul(
                    ps_c[0:NE, 128:256], w_all[:, tt * NE:(tt + 1) * NE],
                    ident_f[:], is_transpose=True, start=True, stop=True), "st")
                B.wait("scalar", "st", ev_cT[tt])
                prev_ctev = B.op("scalar", lambda e, tt=tt: e.activation(
                    combineT[:, tt * 128:(tt + 1) * 128], ps_c[0:NE, 128:256], AF.Copy), "ss")

    B.barrier()

    ev_wj = [None] * 24
    ev_jchunk_done = [None] * 24
    ev_vt_evac_last = None
    psum_free = [None, None, None, None]
    vst_evt = [None, None]
    ev_qkv_ev = {}

    def xn_mv(dc, tblk):
        srct = xn_own if tblk < 2 else xn_nbr
        off = dc * TOWN + (tblk % 2) * 512
        return srct[:, off:off + 512]

    rr = 0
    for jc in range(24):
        if jc >= 2:
            B.wait("gpsimd", "st", ev_jchunk_done[jc - 2])
        ev_wj[jc] = B.dma("gpsimd", wjbuf[jc % 2][:],
                          dwqkv[jc].rearrange("p a b -> p (a b)"), f"swj{jc % 2}")
        B.wait("tensor", f"swj{jc % 2}", ev_wj[jc])
        tblks = [0, 1] if jc < 16 else [0, 1, 2, 3]
        g = jc - 20 if jc >= 20 else (jc - 16 if jc >= 16 else None)
        for tblk in tblks:
            bank = rr % 4
            rr += 1
            B.wait("tensor", "ss", psum_free[bank])

            def qkv_mm(e, jc=jc, tblk=tblk, bank=bank):
                for dc in range(DC):
                    ins = e.matmul(ps_a[:, bank * 512:(bank + 1) * 512],
                                   wjbuf[jc % 2][:, dc * 128:(dc + 1) * 128],
                                   xn_mv(dc, tblk),
                                   start=(dc == 0), stop=(dc == DC - 1))
                return ins
            ev_mm = B.op("tensor", qkv_mm, "st")
            ev_jchunk_done[jc] = ev_mm
            if jc < 16:
                ddst = q_sb[:, jc * TOWN + tblk * 512: jc * TOWN + tblk * 512 + 512]
            elif jc < 20:
                ddst = k_sb[:, g * S + tblk * 512: g * S + tblk * 512 + 512]
            else:
                ddst = vstage[g % 2][:, tblk * 512: tblk * 512 + 512]
            B.wait("scalar", "st", ev_mm)
            if jc >= 20 and tblk == 0 and g >= 2:
                B.wait("scalar", "st", vst_evt[g % 2])
            if jc < 16:
                ev_qkv_ev[(jc, tblk)] = B.op(
                    "scalar", lambda e, d=ddst, bank=bank, jc=jc:
                    e.activation(d, ps_a[:, bank * 512:(bank + 1) * 512],
                                 AF.Identity, bias=bqkv128[:, jc:jc + 1],
                                 scale=1.0 / HD), "ss")
            else:
                ev_qkv_ev[(jc, tblk)] = B.op(
                    "scalar", lambda e, d=ddst, bank=bank, jc=jc:
                    e.activation(d, ps_a[:, bank * 512:(bank + 1) * 512],
                                 AF.Identity, bias=bqkv_sb[:, jc:jc + 1]), "ss")
            psum_free[bank] = ev_qkv_ev[(jc, tblk)]
        if jc >= 20:
            B.wait("tensor", "ss", ev_qkv_ev[(jc, 3)])
            for k in range(2):
                if ev_vt_evac_last is not None:
                    B.wait("tensor", "sv", ev_vt_evac_last)
                elif True:
                    pass

                def vt_batch(e, g=g, k=k):
                    for j in range(8):
                        kc = 8 * k + j
                        ins = e.matmul(ps_b[:, j * 128:(j + 1) * 128],
                                       vstage[g % 2][:, kc * 128:(kc + 1) * 128],
                                       ident_b[:], is_transpose=True,
                                       start=(j == 0), stop=(j == 7))
                    return ins
                evt = B.op("tensor", vt_batch, "st")
                vst_evt[g % 2] = evt
                B.wait("vector", "st", evt)
                dstv = vt_sb[:, (g * 16 + 8 * k) * 128:(g * 16 + 8 * k) * 128 + 1024]
                ev_vt_evac_last = B.op("vector", lambda e, d=dstv:
                                       e.tensor_copy(d, ps_b[:]), "sv")

    B.barrier()

    ev_expC = [None] * 128
    ev_normC = [None] * 128
    ev_biasC = [None] * 128
    ev_tpC = {}
    ev_pvmm = {}
    ev_pvout = {}
    psb_gen = [None, None]
    B.op("vector", lambda e: e.memset(nmax2[0][:], 0.0), "sv")
    B.op("vector", lambda e: e.memset(nmax2[1][:], 0.0), "sv")
    ev_nz = B.cnt["sv"]

    def emit_scores(i):
        h, q = divmod(i, 8)
        g = h % G
        if i >= 1:
            B.wait("tensor", "ss", ev_expC[i - 1])

        def sc_mm(e, h=h, q=q, g=g):
            for kb in range(4):
                ins = e.matmul(ps_a[:, kb * 512:(kb + 1) * 512],
                               q_sb[:, h * TOWN + q * 128: h * TOWN + q * 128 + 128],
                               k_sb[:, g * S + kb * 512: g * S + kb * 512 + 512],
                               start=True, stop=True)
            return ins
        return B.op("tensor", sc_mm, "st")

    def emit_softmax(i, ev_sc):
        qb2 = i % 2
        B.wait("scalar", "st", ev_sc)
        if i >= 2:
            B.wait("scalar", "sv", ev_normC[i - 2])
            B.wait("scalar", "sv", ev_biasC[i - 2])
        else:
            B.wait("scalar", "sv", ev_nz)
        ev_expC[i] = B.op("scalar", lambda e, qb2=qb2: e.activation(
            p_exp[qb2][:], ps_a[:], AF.Exp, bias=nmax2[qb2][:],
            accum_out=ssum2[qb2][:]), "ss")
        ev_ln = B.op("scalar", lambda e, qb2=qb2: e.activation(
            lns2[qb2][:], ssum2[qb2][:], AF.Ln), "ss")
        B.wait("vector", "ss", ev_ln)
        ev_biasC[i] = B.op("vector", lambda e, qb2=qb2: e.tensor_tensor(
            nmax2[qb2][:], nmax2[qb2][:], lns2[qb2][:], ALU.subtract), "sv")
        B.op("vector", lambda e, qb2=qb2: e.reciprocal(rssum[:], ssum2[qb2][:]))
        if (i - 2, 1) in ev_tpC:
            B.wait("vector", ev_tpC[(i - 2, 1)][0], ev_tpC[(i - 2, 1)][1])
        ev_normC[i] = B.op("vector", lambda e, qb2=qb2: e.tensor_scalar_mul(
            pn_sb[qb2][:], p_exp[qb2][:], rssum[:]), "sv")

    def emit_tp(i):
        h, q = divmod(i, 8)
        qb2 = i % 2
        B.wait("tensor", "sv", ev_normC[i])
        for k in range(2):
            pbank = ps_b if k == 0 else ps_b2
            if psb_gen[k] is not None:
                B.wait("tensor", psb_gen[k][0], psb_gen[k][1])

            def pt_batch(e, k=k, qb2=qb2, pbank=pbank):
                for j in range(8):
                    kc = 8 * k + j
                    ins = e.matmul(pbank[:, j * 128:(j + 1) * 128],
                                   pn_sb[qb2][:, kc * 128:(kc + 1) * 128],
                                   ident_b[:], is_transpose=True,
                                   start=(j == 0), stop=(j == 7))
                return ins
            ev_pt = B.op("tensor", pt_batch, "st")
            ev_tpC[(i, k)] = ("st", ev_pt)
            dst_view = pnT.rearrange("p (c q2) -> p c q2", c=DC)[
                :, 8 * k:8 * k + 8, q * 128:q * 128 + 128]
            src_view = pbank.rearrange("p (c q2) -> p c q2", c=8)
            B.wait("vector", "st", ev_pt)
            if h >= 1 and q == 0:
                B.wait("vector", "st", ev_pvmm.get((h - 1, 1)))
            evc = B.op("vector", lambda e, d=dst_view, s=src_view:
                       e.tensor_copy(d, s), "sv")
            psb_gen[k] = ("sv", evc)

    def emit_pv(h):
        g = h % G
        B.wait("tensor", psb_gen[0][0], psb_gen[0][1])
        B.wait("tensor", "sv", ev_pv_vlast[0])
        for qb in range(2):
            pv_ps = ps_c if qb == 0 else ps_d[:, 0:512]
            if h >= 1:
                B.wait("tensor", "ss", ev_pvout[(h - 1, qb)])

            def pv_mm(e, h=h, g=g, qb=qb, pv_ps=pv_ps):
                for kc in range(DC):
                    ins = e.matmul(pv_ps,
                                   vt_sb[:, (g * 16 + kc) * 128:(g * 16 + kc) * 128 + 128],
                                   pnT[:, kc * TOWN + qb * 512: kc * TOWN + qb * 512 + 512],
                                   start=(kc == 0), stop=(kc == DC - 1))
                return ins
            ev_pvmm[(h, qb)] = B.op("tensor", pv_mm, "st")
            B.wait("scalar", "st", ev_pvmm[(h, qb)])
            ev_pvout[(h, qb)] = B.op(
                "scalar", lambda e, h=h, qb=qb, pv_ps=pv_ps:
                e.activation(attn_o[:, h * TOWN + qb * 512: h * TOWN + qb * 512 + 512],
                             pv_ps, AF.Copy), "ss")

    ev_pv_vlast = [None]

    def track_vlast(i):
        ev_pv_vlast[0] = ev_tpC_v[0]

    ev_tpC_v = [None]
    for i in range(128):
        ev_sc = emit_scores(i)
        if i >= 2:
            if i % 8 == 2 and i >= 10:
                emit_pv(i // 8 - 1)
            emit_tp(i - 2)
            ev_pv_vlast[0] = B.cnt["sv"]
        emit_softmax(i, ev_sc)
    emit_tp(126)
    emit_tp(127)
    ev_pv_vlast[0] = B.cnt["sv"]
    emit_pv(15)

    B.barrier()

    ev_wout = [None] * DC
    ev_wout_done = [None] * DC
    psum_free = [None, None, None, None]
    rr = 0
    for oc in range(DC):
        if oc >= 2:
            B.wait("gpsimd", "st", ev_wout_done[oc - 2])
        ev_wout[oc] = B.dma("gpsimd", woutbuf[oc % 2][:],
                            dwout[oc].rearrange("p a b -> p (a b)"), f"swo{oc % 2}")
        B.wait("tensor", f"swo{oc % 2}", ev_wout[oc])
        for tblk in range(2):
            bank = rr % 4
            rr += 1
            B.wait("tensor", "ss", psum_free[bank])

            def wo_mm(e, oc=oc, tblk=tblk, bank=bank):
                for dic in range(DC):
                    ins = e.matmul(ps_a[:, bank * 512:(bank + 1) * 512],
                                   woutbuf[oc % 2][:, dic * 128:(dic + 1) * 128],
                                   attn_o[:, dic * TOWN + tblk * 512: dic * TOWN + tblk * 512 + 512],
                                   start=(dic == 0), stop=(dic == DC - 1))
                return ins
            ev_mm = B.op("tensor", wo_mm, "st")
            ev_wout_done[oc] = ev_mm
            B.wait("scalar", "st", ev_mm)
            evv = B.op("scalar", lambda e, oc=oc, tblk=tblk, bank=bank:
                       e.activation(ffn_acc[:, oc * TOWN + tblk * 512: oc * TOWN + tblk * 512 + 512],
                                    ps_a[:, bank * 512:(bank + 1) * 512],
                                    AF.Identity, bias=bout_sb[:, oc:oc + 1]), "ss")
            psum_free[bank] = evv

    B.barrier()

    ev_sel = [None] * NE
    ev_selfree = [None] * NE
    ev_wgu = {}
    ev_wgu_free = {}
    ev_wd = [None] * (2 * NE + 2)
    ev_crep_ev = {}
    ev_crep_psfree = None
    ev_gu_mm = {}
    ev_silu = {}
    ev_h1 = {}
    ev_hs = {}
    ev_dn_mm = {}
    ev_dn_ev = {}

    def emit_sel_dma(e):
        if e >= NE:
            return
        if e >= 2:
            B.wait("sync", "st", ev_selfree[e - 2])
        ev_sel[e] = B.dma("sync", selbuf[e % 2][:], dsel[e], f"ssel{e % 2}")

    def emit_wgu_dma(gp):
        if gp >= NE * ICH:
            return
        e, ip = gp // ICH, gp % ICH
        for part in range(2):
            ic = ip + 6 * part
            bufi = (gp % 2) * 2 + part
            if gp >= 2:
                pe, pip = (gp - 2) // ICH, (gp - 2) % ICH
                B.wait("gpsimd", "st", ev_wgu_free[(pe, pip, part)])
            ev_wgu[(e, ip, part)] = (B.dma(
                "gpsimd", wgubuf[bufi][:],
                dgu[e, ic].rearrange("p a b -> p (a b)"), f"sgu{bufi}"), bufi)

    def emit_wd_dma(gh):
        if gh >= 2 * NE:
            return
        e, hh = gh // 2, gh % 2
        if gh >= 3:
            B.wait("gpsimd", "st", ev_dn_mm[((gh - 3) // 2, ((gh - 3) % 2) * 16 + 15)])
        ev_wd[gh] = B.dma("gpsimd", wdbuf[gh % 3][:],
                          ddown[e, :, hh * 8:(hh + 1) * 8].rearrange("p a b c -> p (a b c)"),
                          f"swd{gh % 3}")

    def emit_down(de):
        B.wait("tensor", "sv", ev_hs[(de, ICH - 1, 1)])
        for dc in range(DC):
            hh = dc // 8
            gh = de * 2 + hh
            if dc % 8 == 0:
                B.wait("tensor", f"swd{gh % 3}", ev_wd[gh])
            for tb in range(2):
                gen = dc * 2 + tb
                slot = gen % 2
                if (de, gen - 2) in ev_dn_ev:
                    B.wait("tensor", "sv", ev_dn_ev[(de, gen - 2)])
                elif de >= 1:
                    B.wait("tensor", "sv", ev_dn_ev[(de - 1, 30 + gen)])

                def dn_mm(e_, de=de, dc=dc, tb=tb, slot=slot, gh=gh):
                    for icc in range(ICH):
                        ins = e_.matmul(
                            ps_d[:, slot * 512:(slot + 1) * 512],
                            wdbuf[gh % 3][:, (dc % 8) * 768 + icc * 128: (dc % 8) * 768 + icc * 128 + 128],
                            hstash[de % 2][:, icc * TOWN + tb * 512: icc * TOWN + tb * 512 + 512],
                            start=(icc == 0), stop=(icc == ICH - 1))
                    return ins
                ev_dn_mm[(de, gen)] = B.op("tensor", dn_mm, "st")
                B.wait("vector", "st", ev_dn_mm[(de, gen)])
                ev_dn_ev[(de, gen)] = B.op(
                    "vector", lambda e_, dc=dc, tb=tb, slot=slot:
                    e_.tensor_tensor(
                        ffn_acc[:, dc * TOWN + tb * 512: dc * TOWN + tb * 512 + 512],
                        ffn_acc[:, dc * TOWN + tb * 512: dc * TOWN + tb * 512 + 512],
                        ps_d[:, slot * 512:(slot + 1) * 512], ALU.add), "sv")
            if dc == 7:
                emit_wd_dma(2 * de + 3)
            if dc == 15:
                emit_wd_dma(2 * de + 4)

    emit_sel_dma(0)
    emit_sel_dma(1)
    emit_wgu_dma(0)
    emit_wgu_dma(1)
    emit_wd_dma(0)
    emit_wd_dma(1)
    emit_wd_dma(2)

    for e in range(NE):
        B.wait("tensor", f"ssel{e % 2}", ev_sel[e])
        for hfi in range(2):
            if ev_crep_psfree is not None:
                B.wait("tensor", "ss", ev_crep_psfree)
            ev_cm = B.op("tensor", lambda e_, e2=e, hfi=hfi: e_.matmul(
                ps_c[:], selbuf[e2 % 2][:],
                combineT[:, hfi * 512:(hfi + 1) * 512],
                start=True, stop=True), "st")
            ev_selfree[e] = ev_cm
            B.wait("scalar", "st", ev_cm)
            if e >= 2 and hfi == 0:
                B.wait("scalar", "sv", ev_hs[(e - 2, ICH - 1, 1)])
            ev_crep_ev[(e, hfi)] = B.op(
                "scalar", lambda e_, e2=e, hfi=hfi:
                e_.activation(c_rep[e2 % 2][:, hfi * 512:(hfi + 1) * 512],
                              ps_c[:], AF.Copy), "ss")
            ev_crep_psfree = ev_crep_ev[(e, hfi)]
        emit_sel_dma(e + 2)

        for ip in range(ICH):
            for part in range(2):
                for tb in range(2):
                    bank = part * 2 + tb
                    gen = (e, ip, part, tb)
                    if ip >= 1 or e >= 1:
                        pe, pip = (e, ip - 1) if ip >= 1 else (e - 1, ICH - 1)
                        if part == 0:
                            B.wait("tensor", "ss", ev_silu.get((pe, pip, tb)))
                        else:
                            B.wait("tensor", "sv", ev_h1.get((pe, pip, tb)))
                    _c, _b = ev_wgu[(e, ip, part)]
                    B.wait("tensor", f"sgu{_b}", _c)
                    bufi = ((e * ICH + ip) % 2) * 2 + part

                    def gu_mm(e_, bufi=bufi, bank=bank, tb=tb):
                        for dc in range(DC):
                            ins = e_.matmul(ps_a[:, bank * 512:(bank + 1) * 512],
                                            wgubuf[bufi][:, dc * 128:(dc + 1) * 128],
                                            xn_own[:, dc * TOWN + tb * 512: dc * TOWN + tb * 512 + 512],
                                            start=(dc == 0), stop=(dc == DC - 1))
                        return ins
                    ev_gu_mm[gen] = B.op("tensor", gu_mm, "st")
                    ev_wgu_free[(e, ip, part)] = ev_gu_mm[gen]
            emit_wgu_dma(e * ICH + ip + 2)
            for tb in range(2):
                B.wait("scalar", "st", ev_gu_mm[(e, ip, 0, tb)])
                pe, pip = (e, ip - 1) if ip >= 1 else (e - 1, ICH - 1)
                if ip >= 1 or e >= 1:
                    B.wait("scalar", "sv", ev_hs.get((pe, pip, tb)))
                ev_silu[(e, ip, tb)] = B.op("scalar", lambda e_, tb=tb: e_.activation(
                    sg_sb[:, tb * 512:(tb + 1) * 512],
                    ps_a[:, tb * 512:(tb + 1) * 512], AF.Silu), "ss")
            for tb in range(2):
                B.wait("vector", "st", ev_gu_mm[(e, ip, 1, tb)])
                B.wait("vector", "ss", ev_silu[(e, ip, tb)])
                if ip == 0 and tb == 0:
                    B.wait("vector", "ss", ev_crep_ev[(e, 1)])
                    if e >= 2:
                        B.wait("vector", "st", ev_dn_mm[(e - 2, 31)])
                ev_h1[(e, ip, tb)] = B.op("vector", lambda e_, tb=tb: e_.tensor_tensor(
                    sg_sb[:, tb * 512:(tb + 1) * 512],
                    ps_a[:, (2 + tb) * 512:(3 + tb) * 512],
                    sg_sb[:, tb * 512:(tb + 1) * 512], ALU.mult), "sv")
                ev_hs[(e, ip, tb)] = B.op("vector", lambda e_, e2=e, ip=ip, tb=tb:
                                          e_.tensor_tensor(
                    hstash[e2 % 2][:, ip * TOWN + tb * 512: ip * TOWN + tb * 512 + 512],
                    sg_sb[:, tb * 512:(tb + 1) * 512],
                    c_rep[e2 % 2][:, tb * 512:(tb + 1) * 512], ALU.mult), "sv")

        if e >= 1:
            emit_down(e - 1)

    emit_down(NE - 1)

    B.barrier()

    ev_xre = [None] * 8
    ev_osb = [None] * 8
    ev_ostore = [None] * 8
    for tc in range(8):
        if tc >= 2:
            B.wait("sync", "sv", ev_osb[tc - 2])
        ev_xre[tc] = B.dma("sync", x_re[tc % 2][:], dx_own[tc * 128:(tc + 1) * 128, :], f"sxr{tc % 2}")
        if tc >= 1:
            B.wait("tensor", "sv", ev_osb[tc - 1])

        def ftp(e, tc=tc):
            for dc in range(DC):
                ins = e.matmul(ps_a[:, dc * 128:(dc + 1) * 128],
                               ffn_acc[:, dc * TOWN + tc * 128: dc * TOWN + tc * 128 + 128],
                               ident_f[:], is_transpose=True,
                               start=(dc % 4 == 0), stop=(dc % 4 == 3))
            return ins
        ev_ftp = B.op("tensor", ftp, "st")
        B.wait("vector", "st", ev_ftp)
        B.wait("vector", f"sxr{tc % 2}", ev_xre[tc])
        if tc >= 2:
            B.wait("vector", f"sou{tc % 2}", ev_ostore[tc - 2])
        ev_osb[tc] = B.op("vector", lambda e, tc=tc: e.tensor_tensor(
            out_sb[tc % 2][:], ps_a[:], x_re[tc % 2][:], ALU.add), "sv")
        B.wait("sync", "sv", ev_osb[tc])
        ev_ostore[tc] = B.dma("sync", dout[tc * 128:(tc + 1) * 128, :], out_sb[tc % 2][:], f"sou{tc % 2}")

    B.wait("sync", "sou0", ev_ostore[6])
    B.wait("sync", "sou1", ev_ostore[7])

    B.finalize()
    return nc


_NC_CACHE = None


def kernel(L0_attn_x, L0_attn_Wqkv, L0_attn_bqkv, L0_attn_Wout, L0_attn_bout,
           L0_ffn_gate_weight, L0_ffn_gate_up_proj, L0_ffn_down_proj,
           L0_ffn_layer_alpha):
    global _NC_CACHE
    x = np.asarray(L0_attn_x, dtype=np.float32)
    Wqkv = np.asarray(L0_attn_Wqkv, dtype=np.float32)
    bqkv = np.asarray(L0_attn_bqkv, dtype=np.float32)
    Wout = np.asarray(L0_attn_Wout, dtype=np.float32)
    bout = np.asarray(L0_attn_bout, dtype=np.float32)
    gate = np.asarray(L0_ffn_gate_weight, dtype=np.float32)
    gup = np.asarray(L0_ffn_gate_up_proj, dtype=np.float32)
    dwn = np.asarray(L0_ffn_down_proj, dtype=np.float32)
    alpha = np.asarray(L0_ffn_layer_alpha, dtype=np.float32)

    wqkv3 = np.ascontiguousarray(
        Wqkv.reshape(DC, 128, 24, 128).transpose(2, 1, 0, 3))
    wout3 = np.ascontiguousarray(
        Wout.reshape(DC, 128, DC, 128).transpose(2, 1, 0, 3))
    gate3 = np.ascontiguousarray(gate.reshape(NE, DC, 128).transpose(2, 1, 0))
    bqkv2 = np.ascontiguousarray(bqkv.reshape(24, 128).T)
    bout2 = np.ascontiguousarray(bout.reshape(DC, 128).T)
    alpha128 = np.ascontiguousarray(
        np.broadcast_to(alpha.reshape(1, 1), (128, 1)))
    gu4 = np.ascontiguousarray(
        gup.transpose(0, 2, 1).reshape(NE, DC, 128, 12, 128).transpose(0, 3, 2, 1, 4))
    down4 = np.ascontiguousarray(
        dwn.transpose(0, 2, 1).reshape(NE, ICH, 128, DC, 128).transpose(0, 2, 3, 1, 4))
    sel3 = np.zeros((NE, NE, 128), dtype=np.float32)
    for e in range(NE):
        sel3[e, e, :] = 1.0
    identm = np.eye(128, dtype=np.float32)

    if _NC_CACHE is None:
        _NC_CACHE = build_program()
    nc = _NC_CACHE

    shared = {"wqkv3": wqkv3, "wout3": wout3, "gate3": gate3, "bqkv2": bqkv2,
              "bout2": bout2, "alpha128": alpha128, "gu4": gu4, "down4": down4,
              "sel3": sel3, "identm": identm}
    in_maps = []
    for c in range(8):
        b, half = c // 2, c % 2
        m = dict(shared)
        m["x_own"] = np.ascontiguousarray(x[b, half * TOWN:(half + 1) * TOWN, :])
        m["x_nbr"] = np.ascontiguousarray(x[b, (1 - half) * TOWN:(2 - half) * TOWN, :])
        in_maps.append(m)

    res = run_bass_kernel_spmd(nc, in_maps, core_ids=list(range(8)))
    out = np.empty((4, S, D), dtype=np.float32)
    for c in range(8):
        b, half = c // 2, c % 2
        out[b, half * TOWN:(half + 1) * TOWN, :] = res.results[c]["o"]
    return out
